# version 33
# baseline (speedup 1.0000x reference)
import numpy as np
from contextlib import ExitStack

import concourse.bass as bass
import concourse.mybir as mybir
from concourse.bass_utils import run_bass_kernel_spmd

F32 = mybir.dt.float32
BF16 = mybir.dt.bfloat16
FP8 = mybir.dt.float8e4
I8 = mybir.dt.int8
AF = mybir.ActivationFunctionType
ALU = mybir.AluOpType
AX = mybir.AxisListType

T, D, HD = 4096, 1024, 4096
TT = 256
DC = D // 128
HC = HD // 128
NW = TT + 2
C127I = float(np.float32(1.0 / 127.0))
CMEAN_D = float(np.float32(1.0 / 1024.0))

ENGINES = ("sync", "tensor", "vector", "scalar", "gpsimd")


class Planner:

    def __init__(self, nc, stk):
        self.nc = nc
        self.stk = stk
        self.sems = {}
        self.cum = {}
        for en in ENGINES:
            self._mk(en)
        self.plan = {en: [] for en in ENGINES}
        self.last_w = {}
        self.readers = {}
        self.known = {en: {} for en in ENGINES}

    def _mk(self, name):
        self.sems[name] = self.stk.enter_context(self.nc.semaphore("sem_" + name))
        self.cum[name] = 0

    def dma_sem(self, name):
        if name not in self.sems:
            self._mk(name)

    @staticmethod
    def _norm(r):
        if r.startswith("pb") and len(r) > 3 and r[2].isdigit():
            return r[:3]
        return r

    def op(self, en, emit, reads=(), writes=(), sem=None, dma=False,
           extra_deps=(), accum=False):
        sem = sem or en
        reads = tuple(self._norm(r) for r in reads)
        writes = tuple(self._norm(w) for w in writes)
        deps = list(extra_deps)
        for r in reads:
            t = self.last_w.get(r)
            if t is not None:
                deps.append(t)
        for w in writes:
            t = self.last_w.get(w)
            if t is not None:
                deps.append(t)
            deps.extend(self.readers.get(w, ()))
        need = {}
        for (s2, v2) in deps:
            if accum and s2 == en:
                continue
            if v2 > self.known[en].get(s2, 0) and v2 > need.get(s2, 0):
                need[s2] = v2
        for s2, v2 in need.items():
            self.known[en][s2] = v2
        tok = None
        if emit is not None:
            self.cum[sem] += 16 if dma else 1
            tok = (sem, self.cum[sem])
        self.plan[en].append((need, emit, sem, 16 if dma else 1))
        if tok is not None:
            for r in reads:
                self.readers.setdefault(r, []).append(tok)
            for w in writes:
                self.last_w[w] = tok
                self.readers[w] = []
        return tok

    def barrier(self):
        snap = [(s, v) for s, v in self.cum.items() if v > 0]
        for en in ENGINES:
            self.op(en, None, extra_deps=snap)

    def emit_all(self, block):
        def runner(en):
            def body(h):
                for need, emit, sem, amt in self.plan[en]:
                    for s2, v2 in need.items():
                        h.wait_ge(self.sems[s2], v2)
                    if emit is not None:
                        emit(h).then_inc(self.sems[sem], amt)
            return body
        block.sync(runner("sync"))
        block.tensor(runner("tensor"))
        block.vector(runner("vector"))
        block.scalar(runner("scalar"))
        block.gpsimd(runner("gpsimd"))


def build_program(T=T, stage="full"):
    NT = T // TT
    CMEAN_W = float(np.float32(1.0 / (4096.0 * 1024.0)))
    nc = bass.Bass("TRN2", target_bir_lowering=False, dynamic_dma_scratch_size=2048)
    x_d = nc.dram_tensor("L0_attn_x", [T, D], F32, kind="ExternalInput").ap()
    wz_d = nc.dram_tensor("L0_attn_Wz", [D, D], F32, kind="ExternalInput").ap()
    bz_d = nc.dram_tensor("L0_attn_bz", [D], F32, kind="ExternalInput").ap()
    wh_d = nc.dram_tensor("L0_attn_Wh", [D, D], F32, kind="ExternalInput").ap()
    bh_d = nc.dram_tensor("L0_attn_bh", [D], F32, kind="ExternalInput").ap()
    w1_d = nc.dram_tensor("L0_ffn_w1", [HD, D], F32, kind="ExternalInput").ap()
    cw_d = nc.dram_tensor("L0_ffn_conv_w", [HD, 1, 3], F32, kind="ExternalInput").ap()
    cb_d = nc.dram_tensor("L0_ffn_conv_b", [HD], F32, kind="ExternalInput").ap()
    w2_d = nc.dram_tensor("L0_ffn_w2", [D, HD], F32, kind="ExternalInput").ap()
    id_d = nc.dram_tensor("ident", [128, 128], F32, kind="ExternalInput").ap()
    sel_d = nc.dram_tensor("sel2", [2, 128], F32, kind="ExternalInput").ap()
    out_d = nc.dram_tensor("out", [T, D], F32, kind="ExternalOutput").ap()
    yscr_d = nc.dram_tensor("y_scr", [D, T], F32, kind="Internal").ap()
    yscr_r = yscr_d.rearrange("(c p) t -> p c t", p=128)

    with ExitStack() as stk:
        pl = Planner(nc, stk)
        for s in ("dx0", "dx1", "dy0", "dy1", "dyr0", "dyr1", "dyr2", "dow0", "dow1", "dwp0", "dwp1", "dwa0", "dwa1", "dcst"):
            pl.dma_sem(s)

        pb = [stk.enter_context(nc.psum_tensor(f"pbank{i}", [128, 512], F32)) for i in range(8)]
        PB = [f"pb{i}" for i in range(8)]

        identt = stk.enter_context(nc.sbuf_tensor("identt", [128, 128], F32))
        sel2 = stk.enter_context(nc.sbuf_tensor("sel2t", [2, 128], F32))
        ones1 = stk.enter_context(nc.sbuf_tensor("ones1", [1, 128], F32))
        onesb = stk.enter_context(nc.sbuf_tensor("onesb", [128, 1], BF16))
        onesf = stk.enter_context(nc.sbuf_tensor("onesf", [128, 1], F32))
        bzt = stk.enter_context(nc.sbuf_tensor("bzt", [128, DC], F32))
        bht = stk.enter_context(nc.sbuf_tensor("bht", [128, DC], F32))
        cbt = stk.enter_context(nc.sbuf_tensor("cbt", [128, HC], F32))
        cwt = stk.enter_context(nc.sbuf_tensor("cwt", [128, HC, 3], F32))
        cols = stk.enter_context(nc.sbuf_tensor("cols", [128, 16], F32))
        scol = stk.enter_context(nc.sbuf_tensor("scol", [128, 4], F32))
        wsum = stk.enter_context(nc.sbuf_tensor("wsum", [128, HC], F32))
        wtmp = stk.enter_context(nc.sbuf_tensor("wtmp", [128, HC], F32))
        scl1 = stk.enter_context(nc.sbuf_tensor("scl1", [1, 4], F32))
        qrow2 = stk.enter_context(nc.sbuf_tensor("qrow2", [2, NW], F32))
        lncols = stk.enter_context(nc.sbuf_tensor("lncols", [128, 8], F32))
        lnrow2 = stk.enter_context(nc.sbuf_tensor("lnrow2", [2, TT], F32))
        tld = stk.enter_context(nc.sbuf_tensor("tld", [1, 2], F32))


        pl.op("sync", lambda h: h.dma_start(out=identt[:], in_=id_d), writes=("identt",), sem="dcst", dma=True)
        pl.op("sync", lambda h: h.dma_start(out=sel2[:], in_=sel_d), writes=("sel2",), sem="dcst", dma=True)
        pl.op("vector", lambda h: h.memset(ones1[:], 1.0), writes=("ones1",))
        pl.op("vector", lambda h: h.memset(cols[:, 0:16], 1.0), writes=("qcm", "qmc", "qrec", "qsc", "qdq", "colsum"))
        pl.op("vector", lambda h: h.memset(onesb[:], 1.0), writes=("onesb",))
        pl.op("vector", lambda h: h.memset(onesf[:], 1.0), writes=("onesf",))
        pl.op("vector", lambda h: h.memset(tld[:], 1.0), writes=("tld0", "tld1"))
        bz_r = bz_d.rearrange("(c p) -> c p", p=128)
        bh_r = bh_d.rearrange("(c p) -> c p", p=128)
        cb_r = cb_d.rearrange("(c p) -> c p", p=128)
        cw_r = cw_d.rearrange("(c p) one k -> c p (one k)", p=128)
        for c in range(DC):
            pl.op("sync", lambda h, c=c: h.dma_start(out=bzt[:, c:c + 1], in_=bz_r[c:c + 1, :]),
                  writes=(f"bzt{c}",), sem="dcst", dma=True)
            pl.op("sync", lambda h, c=c: h.dma_start(out=bht[:, c:c + 1], in_=bh_r[c:c + 1, :]),
                  writes=(f"bht{c}",), sem="dcst", dma=True)
        for c in range(HC):
            pl.op("sync", lambda h, c=c: h.dma_start(out=cbt[:, c:c + 1], in_=cb_r[c:c + 1, :]),
                  writes=(f"cbt{c}",), sem="dcst", dma=True)
            pl.op("sync", lambda h, c=c: h.dma_start(out=cwt[:, c, :], in_=cw_r[c, :, :]),
                  writes=(f"cwt{c}",), sem="dcst", dma=True)
        _tot = ("dcst", pl.cum["dcst"])
        for _r in ["identt", "sel2"] + [f"bzt{c}" for c in range(DC)] + [f"bht{c}" for c in range(DC)] \
                + [f"cbt{c}" for c in range(HC)] + [f"cwt{c}" for c in range(HC)]:
            pl.last_w[_r] = _tot

        stPrep = ExitStack()
        wslA = [stPrep.enter_context(nc.sbuf_tensor(f"wslA{s}", [128, 1024], F32, side="right")) for s in range(2)]

        def stat_unit(ci):
            s = ci % 2
            pl.op("sync", lambda h, s=s, ci=ci:
                  h.dma_start(out=wslA[s][:, :], in_=w1_d[ci * 128:(ci + 1) * 128, :]),
                  writes=(f"wslA{s}",), sem=f"dwp{s}", dma=True)
            v1 = wslA[s][:, :].rearrange("p (a b) -> p a b", b=128)
            pl.op("vector", lambda h, s=s, v1=v1:
                  h.tensor_reduce(wtmp[:, 0:8], v1, AX.X, ALU.add, apply_absolute_value=True),
                  reads=(f"wslA{s}",), writes=("wtmp",))
            pl.op("vector", lambda h, ci=ci:
                  h.tensor_reduce(wsum[:, ci:ci + 1], wtmp[:, 0:8], AX.X, ALU.add),
                  reads=("wtmp",), writes=("wsum",))

        stat_units = [(ci,) for ci in range(HC)]
        NSU = len(stat_units)

        stA = ExitStack()
        WzT = stA.enter_context(nc.sbuf_tensor("WzT", [128, DC, D], BF16))
        WhT = stA.enter_context(nc.sbuf_tensor("WhT", [128, DC, D], BF16))
        with nc.sbuf_tensor("wsl0", [128, 1024], F32) as wsl0, \
             nc.sbuf_tensor("wsl1", [128, 1024], F32) as wsl1:
            wsl = [wsl0, wsl1]
            for wi, (w_src, wdst, wdn) in enumerate([(wz_d, WzT, "WzT"), (wh_d, WhT, "WhT")]):
                for ec in range(DC):
                    s = (wi * DC + ec) % 2
                    pl.op("sync", lambda h, s=s, ec=ec, w_src=w_src:
                          h.dma_start(out=wsl[s][:, :], in_=w_src[ec * 128:(ec + 1) * 128, :]),
                          writes=(f"wsl{s}",), sem=f"dwp{s}", dma=True)
                    for kc in range(DC):
                        q = kc % 2
                        pl.op("tensor", lambda h, s=s, kc=kc, q=q:
                              h.transpose(pb[q][:, 0:128], wsl[s][:, kc * 128:(kc + 1) * 128], identt[:]),
                              reads=(f"wsl{s}", "identt"), writes=(PB[q],))
                        pl.op("scalar", lambda h, wdst=wdst, kc=kc, ec=ec, q=q:
                              h.activation(wdst[:, kc, ec * 128:(ec + 1) * 128], pb[q][:, 0:128],
                                           AF.Copy, bias=0.0, scale=1.0),
                              reads=(PB[q],), writes=(wdn,))

        pl.barrier()
        if stage == "prep1":
            stA.close()
            with nc.Block() as block:
                pl.emit_all(block)
            return nc

        stAw = ExitStack()
        xtm = [stAw.enter_context(nc.sbuf_tensor(f"xtm{s}", [128, 2, 1024], F32)) for s in range(2)]
        xfm = [stAw.enter_context(nc.sbuf_tensor(f"xfm{s}", [128, DC, TT], F32)) for s in range(3)]
        xbf = [stAw.enter_context(nc.sbuf_tensor(f"xbf{s}", [128, DC, TT], BF16)) for s in range(3)]
        zt = [stAw.enter_context(nc.sbuf_tensor(f"zt{s}", [128, DC, TT], F32)) for s in range(3)]
        sc1 = [stAw.enter_context(nc.sbuf_tensor(f"sc1{s}", [128, DC, TT], F32)) for s in range(3)]
        bt = [stAw.enter_context(nc.sbuf_tensor(f"bt{s}", [128, DC, TT], F32)) for s in range(3)]
        ht = [stAw.enter_context(nc.sbuf_tensor(f"ht{s}", [128, DC, TT], F32)) for s in range(2)]
        rbt = [stAw.enter_context(nc.sbuf_tensor(f"rbt{s}", [128, DC, TT], BF16)) for s in range(2)]
        r2bt = [stAw.enter_context(nc.sbuf_tensor(f"r2bt{s}", [128, DC, TT], BF16)) for s in range(2)]
        yt = stAw.enter_context(nc.sbuf_tensor("yta", [128, DC, TT], F32))
        bmrs = stAw.enter_context(nc.sbuf_tensor("bmrs", [128, TT], F32))


        def act_dummy(af):
            pl.op("scalar", lambda h, af=af: h.activation(tld[0:1, 1:2], tld[0:1, 0:1], af,
                                                          bias=0.0, scale=1.0),
                  reads=("tld0",), writes=("tld1",))

        def ln_stats_apply(rb_t, r2b_t, rb_n, r2b_n, bank, dsf=None, rb_cn=None, r2b_cn=None):
            rb_cn = rb_cn or (lambda c: rb_n)
            r2b_cn = r2b_cn or (lambda c: r2b_n)
            for c in range(DC):
                for g in range(2):
                    pl.op("tensor", lambda h, g=g, c=c:
                          h.matmul(pb[bank][:, g:g + 1], rb_t[:, c, g * 128:(g + 1) * 128], onesb[:, 0:1],
                                   start=(c == 0 and g == 0), stop=(c == DC - 1 and g == 1)),
                          reads=(rb_cn(c), "onesb"), writes=(PB[bank],), accum=(not (c == 0 and g == 0)))
            for c in range(DC):
                for g in range(2):
                    pl.op("tensor", lambda h, g=g, c=c:
                          h.matmul(pb[bank][:, 2 + g:3 + g], r2b_t[:, c, g * 128:(g + 1) * 128], onesb[:, 0:1],
                                   start=(c == 0 and g == 0), stop=(c == DC - 1 and g == 1)),
                          reads=(r2b_cn(c), "onesb"), writes=(PB[bank],), accum=(not (c == 0 and g == 0)))
            pl.op("vector", lambda h: h.tensor_scalar(lncols[:, 0:2], pb[bank][:, 0:2], CMEAN_D, None, ALU.mult),
                  reads=(PB[bank],), writes=("ln_m",))
            pl.op("vector", lambda h: h.tensor_tensor(lncols[:, 6:8], lncols[:, 0:2], lncols[:, 0:2], ALU.mult),
                  reads=("ln_m",), writes=("ln_msq",))
            pl.op("vector", lambda h: h.scalar_tensor_tensor(lncols[:, 6:8], pb[bank][:, 2:4], CMEAN_D,
                                                             lncols[:, 6:8], ALU.mult, ALU.subtract),
                  reads=(PB[bank], "ln_msq"), writes=("ln_var",))
            pl.op("vector", lambda h: h.tensor_scalar(lncols[:, 6:8], lncols[:, 6:8], 1e-5, None, ALU.add),
                  reads=("ln_var",), writes=("ln_varp",))
            pl.op("scalar", lambda h: h.activation(lncols[:, 6:8], lncols[:, 6:8], AF.Sqrt, bias=0.0, scale=1.0),
                  reads=("ln_varp",), writes=("ln_sd",))
            if dsf is not None:
                act_dummy(dsf)
            pl.op("vector", lambda h: h.reciprocal(lncols[:, 2:6:2], lncols[:, 6:8]),
                  reads=("ln_sd",), writes=("ln_rstd",))
            pl.op("vector", lambda h: h.tensor_tensor(lncols[:, 3:7:2], lncols[:, 0:2], lncols[:, 2:6:2], ALU.mult),
                  reads=("ln_m", "ln_rstd"), writes=("ln_mrstd",))
            for g in range(2):
                pl.op("tensor", lambda h, g=g:
                      h.transpose(pb[bank][0:2, 8 + g * 130:8 + g * 130 + 128], lncols[:, 2 + 2 * g:4 + 2 * g], identt[:]),
                      reads=("ln_rstd", "ln_mrstd", "identt"), writes=(PB[bank],), accum=(g > 0))
            for g in range(2):
                pl.op("vector", lambda h, g=g:
                      h.tensor_copy(lnrow2[0:2, g * 128:(g + 1) * 128], pb[bank][0:2, 8 + g * 130:8 + g * 130 + 128]),
                      reads=(PB[bank],), writes=("lnrow2",))
            pl.op("tensor", lambda h: h.matmul(pb[7][:, 0:TT], ones1[:], lnrow2[0:1, 0:TT], start=True, stop=True),
                  reads=("lnrow2", "ones1"), writes=(PB[7],))
            pl.op("tensor", lambda h: h.matmul(pb[7][:, TT:2 * TT], sel2[:, :], lnrow2[0:2, 0:TT], start=True, stop=True),
                  reads=("lnrow2", "sel2"), writes=(PB[7],))

        def a_front(i):
            s = i % 2
            s3 = i % 3
            a0 = i * TT
            pl.op("scalar", lambda h, s=s, a0=a0:
                  h.dma_start(out=xtm[s][:], in_=x_d[a0:a0 + TT, :].rearrange("(g p) d -> p g d", p=128)),
                  writes=(f"xtm{s}",), sem=f"dx{s}", dma=True)
            for g in range(2):
                for cq in range(2):
                    for j in range(4):
                        c = cq * 4 + j
                        pl.op("tensor", lambda h, s=s, g=g, c=c, j=j:
                              h.transpose(pb[0][:, j * 128:(j + 1) * 128],
                                          xtm[s][:, g, c * 128:(c + 1) * 128], identt[:]),
                              reads=(f"xtm{s}", "identt"), writes=(PB[0],), accum=(j > 0))
                    pview = pb[0][:, 0:512].rearrange("p (j n) -> p j n", n=128)
                    pl.op("scalar", lambda h, s=s, g=g, cq=cq, pview=pview:
                          h.activation(xfm[s3][:, cq * 4:(cq + 1) * 4, g * 128:(g + 1) * 128],
                                       pview, AF.Copy, bias=0.0, scale=1.0),
                          reads=(PB[0],), writes=(f"xfm{s3}",))
                    pl.op("scalar", lambda h, s=s, g=g, cq=cq, pview=pview:
                          h.activation(xbf[s3][:, cq * 4:(cq + 1) * 4, g * 128:(g + 1) * 128],
                                       pview, AF.Copy, bias=0.0, scale=1.0),
                          reads=(PB[0],), writes=(f"xbf{s3}",))
            for mc in range(DC):
                qz = 2 + (mc % 2)
                qh = 4 + (mc % 2)
                for kc in range(DC):
                    pl.op("tensor", lambda h, kc=kc, mc=mc, qz=qz, s=s:
                          h.matmul(pb[qz][:, 0:TT], WzT[:, kc, mc * 128:(mc + 1) * 128], xbf[s3][:, kc, :],
                                   start=(kc == 0), stop=(kc == DC - 1)),
                          reads=(f"xbf{s3}", "WzT"), writes=(PB[qz],), accum=(kc > 0))
                pl.op("scalar", lambda h, mc=mc, qz=qz, s=s:
                      h.activation(zt[s3][:, mc, :], pb[qz][:, 0:TT], AF.Sigmoid,
                                   bias=bzt[:, mc:mc + 1], scale=1.0),
                      reads=(PB[qz], f"bzt{mc}"), writes=(f"zt{s3}_{mc}",))
                for kc in range(DC):
                    pl.op("tensor", lambda h, kc=kc, mc=mc, qh=qh, s=s:
                          h.matmul(pb[qh][:, 0:TT], WhT[:, kc, mc * 128:(mc + 1) * 128], xbf[s3][:, kc, :],
                                   start=(kc == 0), stop=(kc == DC - 1)),
                          reads=(f"xbf{s3}", "WhT"), writes=(PB[qh],), accum=(kc > 0))
                pl.op("vector", lambda h, mc=mc, qh=qh, s=s:
                      h.scalar_tensor_tensor(bt[s3][:, mc, :], pb[qh][:, 0:TT], bht[:, mc:mc + 1],
                                             zt[s3][:, mc, :], ALU.add, ALU.mult),
                      reads=(PB[qh], f"bht{mc}", f"zt{s3}_{mc}"), writes=(f"bt{s3}_{mc}",))
                pl.op("gpsimd", lambda h, mc=mc, s3=s3:
                      h.tensor_scalar(sc1[s3][:, mc, :], zt[s3][:, mc, :], -1.0, 1.0, ALU.mult, ALU.add),
                      reads=(f"zt{s3}_{mc}", f"bt{s3}_{mc}"), writes=(f"sc1{s3}_{mc}",))
                pl.op("gpsimd", lambda h, mc=mc, s3=s3:
                      h.tensor_scalar(zt[s3][:, mc, :], sc1[s3][:, mc, :], 1.0, 1e-8, ALU.mult, ALU.add),
                      reads=(f"sc1{s3}_{mc}",), writes=(f"zt{s3}_{mc}",))
            act_dummy(AF.Square)

        def a_back(i, upto=99):
            s = i % 2
            s3 = i % 3
            a0 = i * TT
            for c in range(DC):
                init = 0.0 if i == 0 else ht[1 - s][:, c, TT - 1:TT]
                rd = (f"zt{s3}_{c}", f"bt{s3}_{c}") + (() if i == 0 else (f"ht{1 - s}",))
                pl.op("vector", lambda h, s=s, c=c, init=init:
                      h.tensor_tensor_scan(ht[s][:, c, :], zt[s3][:, c, :], bt[s3][:, c, :],
                                           init, ALU.mult, ALU.add),
                      reads=rd, writes=(f"ht{s}",))
                pl.op("gpsimd", lambda h, s=s, c=c:
                      h.tensor_tensor(sc1[s3][:, c, :], xfm[s3][:, c, :], ht[s][:, c, :], ALU.add),
                      reads=(f"ht{s}", f"xfm{s3}"), writes=(f"sc1{s3}_{c}",))
                pl.op("scalar", lambda h, s=s, s3=s3, c=c:
                      h.activation(rbt[s][:, c, :], sc1[s3][:, c, :], AF.Copy, bias=0.0, scale=1.0),
                      reads=(f"sc1{s3}_{c}",), writes=(f"rbt{s}", f"rbt{s}_{c}"))
                pl.op("scalar", lambda h, s=s, s3=s3, c=c:
                      h.activation(r2bt[s][:, c, :], rbt[s][:, c, :], AF.Square, bias=0.0, scale=1.0),
                      reads=(f"rbt{s}_{c}",), writes=(f"r2bt{s}", f"r2bt{s}_{c}"))

            act_dummy(AF.Sqrt)
            ln_stats_apply(rbt[s], r2bt[s], f"rbt{s}", f"r2bt{s}", 6, dsf=AF.Sigmoid,
                           rb_cn=lambda c, s=s: f"rbt{s}_{c}", r2b_cn=lambda c, s=s: f"r2bt{s}_{c}")
            pl.op("vector", lambda h: h.tensor_copy(bmrs[:], pb[7][:, TT:2 * TT]),
                  reads=(PB[7],), writes=("bmrs",))
            for c in range(DC):
                pl.op("vector", lambda h, c=c, s=s:
                      h.tensor_tensor(bt[s3][:, c, :], sc1[s3][:, c, :], pb[7][:, 0:TT], ALU.mult),
                      reads=(f"sc1{s3}_{c}", PB[7]), writes=(f"bt{s3}_{c}",))
                pl.op("gpsimd", lambda h, c=c, s=s:
                      h.tensor_tensor(yt[:, c, :], bt[s3][:, c, :], bmrs[:], ALU.subtract),
                      reads=(f"bt{s3}_{c}", "bmrs"), writes=("yta",))
            pl.op("sync", lambda h, a0=a0: h.dma_start(out=yscr_r[:, :, a0:a0 + TT], in_=yt[:]),
                  reads=("yta",), writes=(f"yscr{i}",), sem=f"dy{i % 2}", dma=True)

        upto = 99
        only1 = False
        if stage.startswith("A") and len(stage) > 1:
            upto = int(stage[1:])
            only1 = True
        a_front(0)
        if only1:
            a_back(0, upto)
        else:
            if NT > 1:
                a_front(1)
            for i in range(NT):
                if i + 2 < NT:
                    a_front(i + 2)
                a_back(i)
                nun = (NSU * (i + 1)) // NT - (NSU * i) // NT
                for _ in range(nun):
                    if stat_units:
                        stat_unit(*stat_units.pop(0))

        stAw.close()
        stA.close()
        pl.barrier()
        if stage.startswith("A"):
            with nc.Block() as block:
                pl.emit_all(block)
            return nc

        stBr = ExitStack()
        wq1T = stBr.enter_context(nc.sbuf_tensor("wq1T", [128, DC, HD], FP8))
        wq2T = stBr.enter_context(nc.sbuf_tensor("wq2T", [128, HC, D], FP8))
        diag = stBr.enter_context(nc.sbuf_tensor("diag", [128, HC, 3, 128], BF16))

        for hc in range(HC):
            for k in range(3):
                pl.op("vector",
                      lambda h, hc=hc, k=k: h.tensor_scalar(diag[:, hc, k, :], identt[:],
                                                            cwt[:, hc, k:k + 1], None, ALU.mult),
                      reads=("identt", f"cwt{hc}"), writes=("diag",))

        with nc.sbuf_tensor("wsl0b", [128, 4096], F32) as wsl0b, \
             nc.sbuf_tensor("wsl1b", [128, 4096], F32) as wsl1b, \
             nc.sbuf_tensor("qi8", [128, 2, 512], I8) as qi8:
            wslb = [wsl0b, wsl1b]

            def prep_w(w_src, nchunk, ncols, wqT, wqn, sc_i):
                nkc = ncols // 128
                if sc_i != 0:
                    for ci in range(nchunk):
                        for qt in range(4):
                            s = (ci * 4 + qt) % 2
                            pl.op("scalar", lambda h, s=s, ci=ci, qt=qt, w_src=w_src:
                                  h.dma_start(out=wslA[s][:, :],
                                              in_=w_src[ci * 128:(ci + 1) * 128, qt * 1024:(qt + 1) * 1024]),
                                  writes=(f"wslA{s}",), sem=f"dwa{s}", dma=True)
                            v1 = wslA[s][:, :].rearrange("p (a b) -> p a b", b=128)
                            pl.op("vector", lambda h, s=s, v1=v1, qt=qt:
                                  h.tensor_reduce(wtmp[:, qt * 8:qt * 8 + 8], v1, AX.X, ALU.add,
                                                  apply_absolute_value=True),
                                  reads=(f"wslA{s}",), writes=("wtmp",))
                        pl.op("vector", lambda h, ci=ci:
                              h.tensor_reduce(wsum[:, ci:ci + 1], wtmp[:, 0:32], AX.X, ALU.add),
                              reads=("wtmp",), writes=("wsum",))
                pl.op("vector", lambda h, nchunk=nchunk:
                      h.tensor_reduce(cols[:, 0:1], wsum[:, 0:nchunk], AX.X, ALU.add),
                      reads=("wsum",), writes=("colsum",))
                pl.op("tensor", lambda h: h.matmul(pb[0][0:1, 0:1], onesf[:, 0:1], cols[:, 0:1],
                                                   start=True, stop=True),
                      reads=("colsum", "onesf"), writes=(PB[0],))
                pl.op("vector", lambda h: h.tensor_scalar(scl1[0:1, 0:1], pb[0][0:1, 0:1], CMEAN_W, 1e-5,
                                                          ALU.mult, ALU.max),
                      reads=(PB[0],), writes=("s_mean",))
                pl.op("vector", lambda h: h.reciprocal(scl1[0:1, 1:2], scl1[0:1, 0:1]),
                      reads=("s_mean",), writes=("s_scale",))
                pl.op("tensor", lambda h: h.matmul(pb[0][:, 8:10], ones1[:], scl1[0:1, 0:2],
                                                   start=True, stop=True),
                      reads=("s_mean", "s_scale", "ones1"), writes=(PB[0],))
                pl.op("scalar", lambda h, sc_i=sc_i: h.activation(scol[:, sc_i:sc_i + 2], pb[0][:, 8:10],
                                                                  AF.Copy, bias=0.0, scale=1.0),
                      reads=(PB[0],), writes=(f"scol{sc_i}",))
                for ci in range(nchunk):
                    s = ci % 2
                    pl.op("sync", lambda h, s=s, ci=ci, ncols=ncols, w_src=w_src:
                          h.dma_start(out=wslb[s][:, 0:ncols], in_=w_src[ci * 128:(ci + 1) * 128, :]),
                          writes=(f"wslb{s}",), sem=f"dwp{s}", dma=True)
                    for k4 in range(nkc // 4):
                        q = k4 % 2
                        for j in range(4):
                            kc = k4 * 4 + j
                            pl.op("tensor", lambda h, s=s, kc=kc, q=q, j=j:
                                  h.transpose(pb[1 + q][:, j * 128:(j + 1) * 128],
                                              wslb[s][:, kc * 128:(kc + 1) * 128], identt[:]),
                                  reads=(f"wslb{s}", "identt"), writes=(PB[1 + q],), accum=(j > 0))
                        pl.op("scalar", lambda h, q=q, sc_i=sc_i:
                              h.activation(qi8[:, q, :], pb[1 + q][:, 0:512], AF.Copy,
                                           bias=0.0, scale=scol[:, sc_i + 1:sc_i + 2]),
                              reads=(PB[1 + q], f"scol{sc_i}"), writes=(f"qi8{q}",))
                        pl.op("vector", lambda h, q=q, wqT=wqT, k4=k4, ci=ci, wqn=wqn:
                              h.tensor_scalar(wqT[:, k4 * 4:(k4 + 1) * 4, ci * 128:(ci + 1) * 128],
                                              qi8[:, q, :].rearrange("p (j n) -> p j n", n=128),
                                              1.0, -1.0, ALU.min, ALU.max),
                              reads=(f"qi8{q}",), writes=(wqn,))

            prep_w(w1_d, HC, 1024, wq1T, "wq1T", 0)
            prep_w(w2_d, DC, 4096, wq2T, "wq2T", 2)
        stPrep.close()

        pl.barrier()
        if stage == "prep2":
            stBr.close()
            with nc.Block() as block:
                pl.emit_all(block)
            return nc

        stBw = ExitStack()
        ytile2 = [stBw.enter_context(nc.sbuf_tensor(f"ytile{s}", [128, DC, NW], F32)) for s in range(3)]
        mxt = stBw.enter_context(nc.sbuf_tensor("mxt", [128, NW], F32))
        xq8 = stBw.enter_context(nc.sbuf_tensor("xq8", [128, 2, NW], I8))
        xqbf = stBw.enter_context(nc.sbuf_tensor("xqbf", [128, DC, NW], BF16))
        hbf = stBw.enter_context(nc.sbuf_tensor("hbf", [128, HC, NW], BF16))
        vt = stBw.enter_context(nc.sbuf_tensor("vt", [128, HC, TT], F32))
        vq8 = stBw.enter_context(nc.sbuf_tensor("vq8", [128, 2, TT], I8))
        vqbf = stBw.enter_context(nc.sbuf_tensor("vqbf", [128, HC, TT], BF16))
        ftl = [stBw.enter_context(nc.sbuf_tensor(f"ftb{s}", [128, DC, TT], F32)) for s in range(2)]
        r2t = stBw.enter_context(nc.sbuf_tensor("r2t", [128, DC, TT], F32))
        outfm = r2t
        outtm = stBw.enter_context(nc.sbuf_tensor("outtm", [128, 2, 512], F32))
        bm1sb = stBw.enter_context(nc.sbuf_tensor("bm1sb", [128, NW], F32))
        bdq2sb = stBw.enter_context(nc.sbuf_tensor("bdq2sb", [128, TT], F32))
        bmrs2 = stBw.enter_context(nc.sbuf_tensor("bmrs2", [128, TT], F32))

        def quant_rows(nw, mean_sc_i):
            ngr = (nw + 127) // 128
            for gi in range(ngr):
                lo = gi * 128
                w = min(nw, lo + 128) - lo
                pl.op("tensor", lambda h, lo=lo, w=w, gi=gi:
                      h.transpose(pb[4][0:w, gi * 128:gi * 128 + 128], mxt[:, lo:lo + w], identt[:]),
                      reads=("mxt", "identt"), writes=(PB[4],), accum=(gi > 0))
            nfull = nw // 128
            pl.op("vector", lambda h, nfull=nfull:
                  h.tensor_reduce(cols[:, 4:4 + nfull],
                                  pb[4][:, 0:nfull * 128].rearrange("p (g j) -> p g j", j=128),
                                  AX.X, ALU.max),
                  reads=(PB[4],), writes=("qcm",))
            if nw % 128:
                wl = nw % 128
                pl.op("vector", lambda h, nfull=nfull, wl=wl:
                      h.tensor_reduce(cols[0:wl, 4 + nfull:5 + nfull],
                                      pb[4][0:wl, nfull * 128:nfull * 128 + 128], AX.X, ALU.max),
                      reads=(PB[4],), writes=("qcm",))
            pl.op("vector", lambda h, ngr=ngr:
                  h.tensor_scalar(cols[:, 8:8 + ngr], cols[:, 4:4 + ngr], 1e-5, None, ALU.max),
                  reads=("qcm",), writes=("qmc",))
            pl.op("vector", lambda h, ngr=ngr: h.reciprocal(cols[:, 12:12 + ngr], cols[:, 8:8 + ngr]),
                  reads=("qmc",), writes=("qrec",))
            pl.op("vector", lambda h, ngr=ngr:
                  h.tensor_scalar(cols[:, 0:2 * ngr:2], cols[:, 12:12 + ngr], 127.0, None, ALU.mult),
                  reads=("qrec",), writes=("qsc",))
            pl.op("vector", lambda h, ngr=ngr, mean_sc_i=mean_sc_i:
                  h.tensor_scalar(cols[:, 1:2 * ngr + 1:2], cols[:, 8:8 + ngr], C127I,
                                  scol[:, mean_sc_i:mean_sc_i + 1], ALU.mult, ALU.mult),
                  reads=("qmc", f"scol{mean_sc_i}"), writes=("qdq",))
            for gi in range(ngr):
                lo = gi * 128
                w = min(nw, lo + 128) - lo
                pl.op("tensor", lambda h, gi=gi, w=w:
                      h.transpose(pb[4][0:2, 384 + 0:384 + w] if False else pb[4][0:2, gi * 130:gi * 130 + w],
                                  cols[0:w, 2 * gi:2 * gi + 2], identt[0:w, 0:w]),
                      reads=("qsc", "qdq", "identt"), writes=(PB[4],), accum=(gi > 0))
            for gi in range(ngr):
                lo = gi * 128
                w = min(nw, lo + 128) - lo
                pl.op("vector", lambda h, gi=gi, lo=lo, w=w:
                      h.tensor_copy(qrow2[0:2, lo:lo + w], pb[4][0:2, gi * 130:gi * 130 + w]),
                      reads=(PB[4],), writes=("qrow2",))

        def b_load(i):
            s = i % 3
            ytile = ytile2[s]
            yr = f"ytile{s}"
            if i == 0:
                pl.op("vector", lambda h, ytile=ytile: h.memset(ytile[:, :, 0:1], 0.0), writes=(yr,))
                pl.op("scalar", lambda h, ytile=ytile: h.dma_start(out=ytile[:, :, 1:NW], in_=yscr_r[:, :, 0:TT + 1]),
                      reads=("yscr0", "yscr1"), writes=(yr,), sem=f"dyr{s}", dma=True)
            elif i == NT - 1:
                pl.op("vector", lambda h, ytile=ytile: h.memset(ytile[:, :, NW - 1:NW], 0.0), writes=(yr,))
                pl.op("scalar", lambda h, a0=i * TT, ytile=ytile: h.dma_start(out=ytile[:, :, 0:NW - 1], in_=yscr_r[:, :, a0 - 1:T]),
                      reads=(f"yscr{i - 1}", f"yscr{i}"), writes=(yr,), sem=f"dyr{s}", dma=True)
            else:
                pl.op("scalar", lambda h, a0=i * TT, ytile=ytile: h.dma_start(out=ytile[:], in_=yscr_r[:, :, a0 - 1:a0 + TT + 1]),
                      reads=(f"yscr{i - 1}", f"yscr{i}", f"yscr{i + 1}"), writes=(yr,), sem=f"dyr{s}", dma=True)

        def q2_red_half(half):
            pl.op("vector", lambda h, half=half:
                  h.tensor_reduce(pb[6][:, half * TT:(half + 1) * TT],
                                  vt[:, half * 16:(half + 1) * 16, :].rearrange("p c t -> p t c"),
                                  AX.X, ALU.max, apply_absolute_value=True),
                  reads=tuple(f"vt{hc}" for hc in range(half * 16, (half + 1) * 16)),
                  writes=(PB[6],))

        def q2_combine():
            pl.op("vector", lambda h:
                  h.tensor_reduce(mxt[:, 0:TT], pb[6][:, 0:512].rearrange("p (c t) -> p t c", t=TT),
                                  AX.X, ALU.max),
                  reads=(PB[6],), writes=("mxt",))

        def q2_bcast(i):
            pl.op("tensor", lambda h: h.matmul(pb[7][:, 0:TT], ones1[:], qrow2[0:1, 0:TT], start=True, stop=True),
                  reads=("qrow2", "ones1"), writes=(PB[7],))
            pl.op("tensor", lambda h: h.matmul(pb[4][:, 0:TT], sel2[:, :], qrow2[0:2, 0:TT], start=True, stop=True),
                  reads=("qrow2", "sel2"), writes=(PB[4],))
            pl.op("scalar", lambda h: h.activation(bdq2sb[:], pb[4][:, 0:TT], AF.Copy, bias=0.0, scale=1.0),
                  reads=(PB[4],), writes=("bdq2sb",))

        def plan_vq(i, hc):
            pl.op("vector", lambda h, hc=hc: h.tensor_tensor(vq8[:, hc % 2, :], vt[:, hc, :], pb[7][:, 0:TT], ALU.mult),
                  reads=(f"vt{hc}", PB[7]), writes=(f"vq8_{hc % 2}",))
            pl.op("scalar", lambda h, hc=hc: h.activation(vqbf[:, hc, :], vq8[:, hc % 2, :], AF.Copy, bias=0.0, scale=1.0),
                  reads=(f"vq8_{hc % 2}",), writes=(f"vqbf{hc}",) + (("vqbf_done",) if hc == HC - 1 else ()))

        def bt_m1(i, vq_of=None):
            a0 = i * TT
            s = i % 3
            ytile = ytile2[s]
            yr = f"ytile{s}"
            pl.op("vector", lambda h, ytile=ytile: h.tensor_reduce(mxt[:, 0:NW], ytile[:].rearrange("p c t -> p t c"),
                                                      AX.X, ALU.max, apply_absolute_value=True),
                  reads=(yr,), writes=("mxt",))
            quant_rows(NW, 0)
            pl.op("tensor", lambda h: h.matmul(pb[5][:, 0:NW], ones1[:], qrow2[0:1, 0:NW], start=True, stop=True),
                  reads=("qrow2", "ones1"), writes=(PB[5],))
            pl.op("tensor", lambda h: h.matmul(pb[4][:, 0:NW], sel2[:, :], qrow2[0:2, 0:NW], start=True, stop=True),
                  reads=("qrow2", "sel2"), writes=(PB[4],))
            pl.op("scalar", lambda h: h.activation(bm1sb[:], pb[4][:, 0:NW], AF.Copy, bias=0.0, scale=1.0),
                  reads=(PB[4],), writes=("bm1sb",))
            for c in range(DC):
                pl.op("vector", lambda h, c=c, ytile=ytile: h.tensor_tensor(xq8[:, c % 2, :], ytile[:, c, :], pb[5][:, 0:NW], ALU.mult),
                      reads=(yr, PB[5]), writes=(f"xq8_{c % 2}",))
                pl.op("scalar", lambda h, c=c: h.activation(xqbf[:, c, :], xq8[:, c % 2, :], AF.Copy, bias=0.0, scale=1.0),
                      reads=(f"xq8_{c % 2}",), writes=(f"xqbf{c}",) + (("xqbf_done",) if c == DC - 1 else ()))
            for hc in range(HC):
                q = hc % 2
                for kc in range(DC):
                    pl.op("tensor", lambda h, hc=hc, kc=kc, q=q:
                          h.matmul(pb[q][:, 0:NW], wq1T[:, kc, hc * 128:(hc + 1) * 128], xqbf[:, kc, :],
                                   start=(kc == 0), stop=(kc == DC - 1)),
                          reads=(f"xqbf{kc}", "wq1T"), writes=(PB[q],), accum=(kc > 0))
                pl.op("vector", lambda h, hc=hc, q=q:
                      h.tensor_tensor(hbf[:, hc, :], pb[q][:, 0:NW], bm1sb[:], ALU.mult),
                      reads=(PB[q], "bm1sb"), writes=(f"hbf{hc}",))
                if vq_of is not None:
                    plan_vq(vq_of, hc)
            if i == 0 or i == NT - 1:
                col = slice(0, 1) if i == 0 else slice(NW - 1, NW)
                pl.op("vector", lambda h, col=col: h.memset(hbf[:, :, col], 0.0),
                      writes=tuple(f"hbf{hc}" for hc in range(HC)))
            for hc in range(HC):
                q = hc % 4
                for k in range(3):
                    pl.op("tensor", lambda h, hc=hc, k=k, q=q:
                          h.matmul(pb[q][:, 0:TT], diag[:, hc, k, :], hbf[:, hc, k:k + TT],
                                   start=(k == 0), stop=(k == 2)),
                          reads=(f"hbf{hc}", "diag"), writes=(PB[q],), accum=(k > 0))
                pl.op("scalar", lambda h, hc=hc, q=q:
                      h.activation(vt[:, hc, :], pb[q][:, 0:TT], AF.Silu, bias=cbt[:, hc:hc + 1], scale=1.0),
                      reads=(PB[q], f"cbt{hc}"), writes=(f"vt{hc}",))
            q2_red_half(0)

        def bt_w2(i, standalone_vq=False, q2red=False):
            sf = i % 2
            ft = ftl[sf]
            if standalone_vq:
                for hc in range(HC):
                    plan_vq(i, hc)
            for dc in range(DC):
                q = (5, 4, 2, 3, 5, 4, 2, 3)[dc]
                for kc in range(HC):
                    pl.op("tensor", lambda h, dc=dc, kc=kc, q=q:
                          h.matmul(pb[q][:, 0:TT], wq2T[:, kc, dc * 128:(dc + 1) * 128], vqbf[:, kc, :],
                                   start=(kc == 0), stop=(kc == HC - 1)),
                          reads=(f"vqbf{kc}", "wq2T"), writes=(PB[q],), accum=(kc > 0))
                pl.op("vector", lambda h, dc=dc, q=q, ft=ft:
                      h.tensor_tensor(ft[:, dc, :], pb[q][:, 0:TT], bdq2sb[:], ALU.mult),
                      reads=(PB[q], "bdq2sb"), writes=(f"ft{sf}_{dc}",))
                if dc == 3 and q2red:
                    q2_red_half(1)
                    q2_combine()

        def bt_back(i):
            a0 = i * TT
            s = i % 3
            sf = i % 2
            ft = ftl[sf]
            ytile = ytile2[s]
            yr = f"ytile{s}"
            fdeps = tuple(f"ft{sf}_{dc}" for dc in range(DC))
            r2c = tuple(f"r2tc{c}" for c in range(DC))
            pl.op("gpsimd", lambda h, ytile=ytile, ft=ft: h.tensor_tensor(r2t[:], ytile[:, :, 1:NW - 1], ft[:], ALU.add),
                  reads=fdeps + (yr,), writes=r2c)
            pl.op("scalar", lambda h: h.activation(xqbf[:, :, 0:TT], r2t[:], AF.Copy, bias=0.0, scale=1.0),
                  reads=r2c, writes=("rb2", "xqbf_done") + tuple(f"xqbf{c}" for c in range(DC)))
            act_dummy(AF.Sqrt)
            pl.op("gpsimd", lambda h: h.tensor_tensor(vqbf[:, 0:DC, :], r2t[:], r2t[:], ALU.mult),
                  reads=r2c, writes=("r2b2", "vqbf_done") + tuple(f"vqbf{c}" for c in range(DC)))
            ln_stats_apply(xqbf[:, :, 0:TT].rearrange("p c t -> p c t") if False else xqbf, vqbf, "rb2", "r2b2", 6, dsf=AF.Silu)
            pl.op("vector", lambda h: h.tensor_copy(bmrs2[:], pb[7][:, TT:2 * TT]),
                  reads=(PB[7],), writes=("bmrs2",))
            for c in range(DC):
                pl.op("vector", lambda h, c=c, ft=ft: h.tensor_tensor(ft[:, c, :], r2t[:, c, :], pb[7][:, 0:TT], ALU.mult),
                      reads=(f"r2tc{c}", PB[7]), writes=(f"ft{sf}_{c}",))
                pl.op("gpsimd", lambda h, c=c, ft=ft: h.tensor_tensor(outfm[:, c, :], ft[:, c, :], bmrs2[:],
                                                               ALU.subtract),
                      reads=(f"ft{sf}_{c}", "bmrs2"), writes=(f"r2tc{c}",))
            for g in range(2):
                for cq in range(2):
                    r = g * 2 + cq
                    qb = 4 if r % 2 == 0 else 5
                    for j in range(4):
                        c = cq * 4 + j
                        pl.op("tensor", lambda h, g=g, c=c, j=j, qb=qb:
                              h.transpose(pb[qb][:, j * 128:(j + 1) * 128], outfm[:, c, g * 128:(g + 1) * 128],
                                          identt[:]),
                              reads=(f"r2tc{c}", "identt"), writes=(PB[qb],), accum=(j > 0))
                    pl.op("scalar", lambda h, r=r, qb=qb:
                          h.activation(outtm[:, r % 2, :].rearrange("p (j n) -> p j n", n=128),
                                       pb[qb][:, 0:512].rearrange("p (j n) -> p j n", n=128),
                                       AF.Copy, bias=0.0, scale=1.0),
                          reads=(PB[qb],), writes=(f"outtm{r % 2}",))
                    pl.op("sync", lambda h, a0=a0, g=g, cq=cq, r=r:
                          h.dma_start(out=out_d[a0 + g * 128:a0 + (g + 1) * 128, cq * 512:(cq + 1) * 512],
                                      in_=outtm[:, r % 2, :]),
                          reads=(f"outtm{r % 2}",), writes=(f"outd{i}_{g}_{cq}",), sem=f"dow{r % 2}", dma=True)

        b_load(0)
        if NT > 1:
            b_load(1)
        bt_m1(0)
        q2_red_half(1)
        q2_combine()
        quant_rows(TT, 2)
        for i in range(NT):
            q2_bcast(i)
            if i + 1 < NT:
                bt_m1(i + 1, vq_of=i)
                bt_w2(i, q2red=True)
                quant_rows(TT, 2)
            else:
                bt_w2(i, standalone_vq=True)
            if i > 0:
                bt_back(i - 1)
            if i + 2 < NT:
                b_load(i + 2)
        bt_back(NT - 1)

        stBw.close()
        stBr.close()
        pl.op("sync", None, extra_deps=[("dow0", pl.cum["dow0"]), ("dow1", pl.cum["dow1"])])

        with nc.Block() as block:
            pl.emit_all(block)
    return nc


_NC_CACHE = {}


def kernel(**inputs):
    xs = np.ascontiguousarray(np.asarray(inputs["L0_attn_x"], dtype=np.float32))
    ident = np.eye(128, dtype=np.float32)
    sel2 = np.zeros((2, 128), dtype=np.float32)
    sel2[1, :] = 1.0
    names = ["L0_attn_Wz", "L0_attn_bz", "L0_attn_Wh", "L0_attn_bh",
             "L0_ffn_w1", "L0_ffn_conv_w", "L0_ffn_conv_b", "L0_ffn_w2"]
    ws = {n: np.ascontiguousarray(np.asarray(inputs[n], dtype=np.float32)) for n in names}
    if "nc" not in _NC_CACHE:
        _NC_CACHE["nc"] = build_program()
    nc = _NC_CACHE["nc"]
    in_maps = []
    for c in range(8):
        m = {"L0_attn_x": xs[c], "ident": ident, "sel2": sel2}
        m.update(ws)
        in_maps.append(m)
    res = run_bass_kernel_spmd(nc, in_maps, core_ids=list(range(8)))
    out = np.stack([np.asarray(r["out"]) for r in res.results], axis=0).astype(np.float32)
    return out


# version 34 (final)
# speedup vs baseline: 1.0160x; 1.0160x over previous
import numpy as np
from contextlib import ExitStack

import concourse.bass as bass
import concourse.mybir as mybir
from concourse.bass_utils import run_bass_kernel_spmd

F32 = mybir.dt.float32
BF16 = mybir.dt.bfloat16
FP8 = mybir.dt.float8e4
I8 = mybir.dt.int8
AF = mybir.ActivationFunctionType
ALU = mybir.AluOpType
AX = mybir.AxisListType

T, D, HD = 4096, 1024, 4096
TT = 256
DC = D // 128
HC = HD // 128
NW = TT + 2
C127I = float(np.float32(1.0 / 127.0))
CMEAN_D = float(np.float32(1.0 / 1024.0))

ENGINES = ("sync", "tensor", "vector", "scalar", "gpsimd")


class Planner:

    def __init__(self, nc, stk):
        self.nc = nc
        self.stk = stk
        self.sems = {}
        self.cum = {}
        for en in ENGINES:
            self._mk(en)
        self.plan = {en: [] for en in ENGINES}
        self.last_w = {}
        self.readers = {}
        self.known = {en: {} for en in ENGINES}

    def _mk(self, name):
        self.sems[name] = self.stk.enter_context(self.nc.semaphore("sem_" + name))
        self.cum[name] = 0

    def dma_sem(self, name):
        if name not in self.sems:
            self._mk(name)

    @staticmethod
    def _norm(r):
        if r.startswith("pb") and len(r) > 3 and r[2].isdigit():
            return r[:3]
        return r

    def op(self, en, emit, reads=(), writes=(), sem=None, dma=False,
           extra_deps=(), accum=False):
        sem = sem or en
        reads = tuple(self._norm(r) for r in reads)
        writes = tuple(self._norm(w) for w in writes)
        deps = list(extra_deps)
        for r in reads:
            t = self.last_w.get(r)
            if t is not None:
                deps.append(t)
        for w in writes:
            t = self.last_w.get(w)
            if t is not None:
                deps.append(t)
            deps.extend(self.readers.get(w, ()))
        need = {}
        for (s2, v2) in deps:
            if accum and s2 == en:
                continue
            if v2 > self.known[en].get(s2, 0) and v2 > need.get(s2, 0):
                need[s2] = v2
        for s2, v2 in need.items():
            self.known[en][s2] = v2
        tok = None
        if emit is not None:
            self.cum[sem] += 16 if dma else 1
            tok = (sem, self.cum[sem])
        self.plan[en].append((need, emit, sem, 16 if dma else 1))
        if tok is not None:
            for r in reads:
                self.readers.setdefault(r, []).append(tok)
            for w in writes:
                self.last_w[w] = tok
                self.readers[w] = []
        return tok

    def barrier(self):
        snap = [(s, v) for s, v in self.cum.items() if v > 0]
        for en in ENGINES:
            self.op(en, None, extra_deps=snap)

    def emit_all(self, block):
        def runner(en):
            def body(h):
                for need, emit, sem, amt in self.plan[en]:
                    for s2, v2 in need.items():
                        h.wait_ge(self.sems[s2], v2)
                    if emit is not None:
                        emit(h).then_inc(self.sems[sem], amt)
            return body
        block.sync(runner("sync"))
        block.tensor(runner("tensor"))
        block.vector(runner("vector"))
        block.scalar(runner("scalar"))
        block.gpsimd(runner("gpsimd"))


def build_program(T=T, stage="full"):
    NT = T // TT
    CMEAN_W = float(np.float32(1.0 / (4096.0 * 1024.0)))
    nc = bass.Bass("TRN2", target_bir_lowering=False, dynamic_dma_scratch_size=2048)
    x_d = nc.dram_tensor("L0_attn_x", [T, D], F32, kind="ExternalInput").ap()
    wz_d = nc.dram_tensor("L0_attn_Wz", [D, D], F32, kind="ExternalInput").ap()
    bz_d = nc.dram_tensor("L0_attn_bz", [D], F32, kind="ExternalInput").ap()
    wh_d = nc.dram_tensor("L0_attn_Wh", [D, D], F32, kind="ExternalInput").ap()
    bh_d = nc.dram_tensor("L0_attn_bh", [D], F32, kind="ExternalInput").ap()
    w1_d = nc.dram_tensor("L0_ffn_w1", [HD, D], F32, kind="ExternalInput").ap()
    cw_d = nc.dram_tensor("L0_ffn_conv_w", [HD, 1, 3], F32, kind="ExternalInput").ap()
    cb_d = nc.dram_tensor("L0_ffn_conv_b", [HD], F32, kind="ExternalInput").ap()
    w2_d = nc.dram_tensor("L0_ffn_w2", [D, HD], F32, kind="ExternalInput").ap()
    id_d = nc.dram_tensor("ident", [128, 128], F32, kind="ExternalInput").ap()
    sel_d = nc.dram_tensor("sel2", [2, 128], F32, kind="ExternalInput").ap()
    out_d = nc.dram_tensor("out", [T, D], F32, kind="ExternalOutput").ap()
    yscr_d = nc.dram_tensor("y_scr", [D, T], F32, kind="Internal").ap()
    yscr_r = yscr_d.rearrange("(c p) t -> p c t", p=128)

    with ExitStack() as stk:
        pl = Planner(nc, stk)
        for s in ("dx0", "dx1", "dy0", "dy1", "dyr0", "dyr1", "dyr2", "dow0", "dow1", "dwp0", "dwp1", "dcst"):
            pl.dma_sem(s)

        pb = [stk.enter_context(nc.psum_tensor(f"pbank{i}", [128, 512], F32)) for i in range(8)]
        PB = [f"pb{i}" for i in range(8)]

        identt = stk.enter_context(nc.sbuf_tensor("identt", [128, 128], F32))
        sel2 = stk.enter_context(nc.sbuf_tensor("sel2t", [2, 128], F32))
        ones1 = stk.enter_context(nc.sbuf_tensor("ones1", [1, 128], F32))
        onesb = stk.enter_context(nc.sbuf_tensor("onesb", [128, 1], BF16))
        onesf = stk.enter_context(nc.sbuf_tensor("onesf", [128, 1], F32))
        bzt = stk.enter_context(nc.sbuf_tensor("bzt", [128, DC], F32))
        bht = stk.enter_context(nc.sbuf_tensor("bht", [128, DC], F32))
        cbt = stk.enter_context(nc.sbuf_tensor("cbt", [128, HC], F32))
        cwt = stk.enter_context(nc.sbuf_tensor("cwt", [128, HC, 3], F32))
        cols = stk.enter_context(nc.sbuf_tensor("cols", [128, 16], F32))
        scol = stk.enter_context(nc.sbuf_tensor("scol", [128, 4], F32))
        wsum = stk.enter_context(nc.sbuf_tensor("wsum", [128, HC], F32))
        wtmp = stk.enter_context(nc.sbuf_tensor("wtmp", [128, HC], F32))
        scl1 = stk.enter_context(nc.sbuf_tensor("scl1", [1, 4], F32))
        qrow2 = stk.enter_context(nc.sbuf_tensor("qrow2", [2, NW], F32))
        lncols = stk.enter_context(nc.sbuf_tensor("lncols", [128, 8], F32))
        lnrow2 = stk.enter_context(nc.sbuf_tensor("lnrow2", [2, TT], F32))
        tld = stk.enter_context(nc.sbuf_tensor("tld", [1, 2], F32))


        pl.op("sync", lambda h: h.dma_start(out=identt[:], in_=id_d), writes=("identt",), sem="dcst", dma=True)
        pl.op("sync", lambda h: h.dma_start(out=sel2[:], in_=sel_d), writes=("sel2",), sem="dcst", dma=True)
        pl.op("vector", lambda h: h.memset(ones1[:], 1.0), writes=("ones1",))
        pl.op("vector", lambda h: h.memset(cols[:, 0:16], 1.0), writes=("qcm", "qmc", "qrec", "qsc", "qdq", "colsum"))
        pl.op("vector", lambda h: h.memset(onesb[:], 1.0), writes=("onesb",))
        pl.op("vector", lambda h: h.memset(onesf[:], 1.0), writes=("onesf",))
        pl.op("vector", lambda h: h.memset(tld[:], 1.0), writes=("tld0", "tld1"))
        bz_r = bz_d.rearrange("(c p) -> c p", p=128)
        bh_r = bh_d.rearrange("(c p) -> c p", p=128)
        cb_r = cb_d.rearrange("(c p) -> c p", p=128)
        cw_r = cw_d.rearrange("(c p) one k -> c p (one k)", p=128)
        for c in range(DC):
            pl.op("sync", lambda h, c=c: h.dma_start(out=bzt[:, c:c + 1], in_=bz_r[c:c + 1, :]),
                  writes=(f"bzt{c}",), sem="dcst", dma=True)
            pl.op("sync", lambda h, c=c: h.dma_start(out=bht[:, c:c + 1], in_=bh_r[c:c + 1, :]),
                  writes=(f"bht{c}",), sem="dcst", dma=True)
        for c in range(HC):
            pl.op("sync", lambda h, c=c: h.dma_start(out=cbt[:, c:c + 1], in_=cb_r[c:c + 1, :]),
                  writes=(f"cbt{c}",), sem="dcst", dma=True)
            pl.op("sync", lambda h, c=c: h.dma_start(out=cwt[:, c, :], in_=cw_r[c, :, :]),
                  writes=(f"cwt{c}",), sem="dcst", dma=True)
        _tot = ("dcst", pl.cum["dcst"])
        for _r in ["identt", "sel2"] + [f"bzt{c}" for c in range(DC)] + [f"bht{c}" for c in range(DC)] \
                + [f"cbt{c}" for c in range(HC)] + [f"cwt{c}" for c in range(HC)]:
            pl.last_w[_r] = _tot

        stPrep = ExitStack()
        wslA = [stPrep.enter_context(nc.sbuf_tensor(f"wslA{s}", [128, 1024], F32, side="right")) for s in range(2)]

        def stat_unit(ci):
            s = ci % 2
            pl.op("sync", lambda h, s=s, ci=ci:
                  h.dma_start(out=wslA[s][:, :], in_=w1_d[ci * 128:(ci + 1) * 128, :]),
                  writes=(f"wslA{s}",), sem=f"dwp{s}", dma=True)
            v1 = wslA[s][:, :].rearrange("p (a b) -> p a b", b=128)
            pl.op("vector", lambda h, s=s, v1=v1:
                  h.tensor_reduce(wtmp[:, 0:8], v1, AX.X, ALU.add, apply_absolute_value=True),
                  reads=(f"wslA{s}",), writes=("wtmp",))
            pl.op("vector", lambda h, ci=ci:
                  h.tensor_reduce(wsum[:, ci:ci + 1], wtmp[:, 0:8], AX.X, ALU.add),
                  reads=("wtmp",), writes=("wsum",))

        stat_units = [(ci,) for ci in range(HC)]
        NSU = len(stat_units)

        stA = ExitStack()
        WzT = stA.enter_context(nc.sbuf_tensor("WzT", [128, DC, D], BF16))
        WhT = stA.enter_context(nc.sbuf_tensor("WhT", [128, DC, D], BF16))
        with nc.sbuf_tensor("wsl0", [128, 1024], F32) as wsl0, \
             nc.sbuf_tensor("wsl1", [128, 1024], F32) as wsl1:
            wsl = [wsl0, wsl1]
            for wi, (w_src, wdst, wdn) in enumerate([(wz_d, WzT, "WzT"), (wh_d, WhT, "WhT")]):
                for ec in range(DC):
                    s = (wi * DC + ec) % 2
                    pl.op("sync", lambda h, s=s, ec=ec, w_src=w_src:
                          h.dma_start(out=wsl[s][:, :], in_=w_src[ec * 128:(ec + 1) * 128, :]),
                          writes=(f"wsl{s}",), sem=f"dwp{s}", dma=True)
                    for kc in range(DC):
                        q = kc % 2
                        pl.op("tensor", lambda h, s=s, kc=kc, q=q:
                              h.transpose(pb[q][:, 0:128], wsl[s][:, kc * 128:(kc + 1) * 128], identt[:]),
                              reads=(f"wsl{s}", "identt"), writes=(PB[q],))
                        pl.op("scalar", lambda h, wdst=wdst, kc=kc, ec=ec, q=q:
                              h.activation(wdst[:, kc, ec * 128:(ec + 1) * 128], pb[q][:, 0:128],
                                           AF.Copy, bias=0.0, scale=1.0),
                              reads=(PB[q],), writes=(wdn,))

        pl.barrier()
        if stage == "prep1":
            stA.close()
            with nc.Block() as block:
                pl.emit_all(block)
            return nc

        stAw = ExitStack()
        xtm = [stAw.enter_context(nc.sbuf_tensor(f"xtm{s}", [128, 2, 1024], F32)) for s in range(2)]
        xfm = [stAw.enter_context(nc.sbuf_tensor(f"xfm{s}", [128, DC, TT], F32)) for s in range(3)]
        xbf = [stAw.enter_context(nc.sbuf_tensor(f"xbf{s}", [128, DC, TT], BF16)) for s in range(3)]
        zt = [stAw.enter_context(nc.sbuf_tensor(f"zt{s}", [128, DC, TT], F32)) for s in range(3)]
        sc1 = [stAw.enter_context(nc.sbuf_tensor(f"sc1{s}", [128, DC, TT], F32)) for s in range(3)]
        bt = [stAw.enter_context(nc.sbuf_tensor(f"bt{s}", [128, DC, TT], F32)) for s in range(3)]
        ht = [stAw.enter_context(nc.sbuf_tensor(f"ht{s}", [128, DC, TT], F32)) for s in range(2)]
        rbt = [stAw.enter_context(nc.sbuf_tensor(f"rbt{s}", [128, DC, TT], BF16)) for s in range(2)]
        r2bt = [stAw.enter_context(nc.sbuf_tensor(f"r2bt{s}", [128, DC, TT], BF16)) for s in range(2)]
        yt = stAw.enter_context(nc.sbuf_tensor("yta", [128, DC, TT], F32))
        bmrs = stAw.enter_context(nc.sbuf_tensor("bmrs", [128, TT], F32))


        def act_dummy(af):
            pl.op("scalar", lambda h, af=af: h.activation(tld[0:1, 1:2], tld[0:1, 0:1], af,
                                                          bias=0.0, scale=1.0),
                  reads=("tld0",), writes=("tld1",))

        def ln_stats_apply(rb_t, r2b_t, rb_n, r2b_n, bank, dsf=None, rb_cn=None, r2b_cn=None):
            rb_cn = rb_cn or (lambda c: rb_n)
            r2b_cn = r2b_cn or (lambda c: r2b_n)
            for c in range(DC):
                for g in range(2):
                    pl.op("tensor", lambda h, g=g, c=c:
                          h.matmul(pb[bank][:, g:g + 1], rb_t[:, c, g * 128:(g + 1) * 128], onesb[:, 0:1],
                                   start=(c == 0 and g == 0), stop=(c == DC - 1 and g == 1)),
                          reads=(rb_cn(c), "onesb"), writes=(PB[bank],), accum=(not (c == 0 and g == 0)))
            for c in range(DC):
                for g in range(2):
                    pl.op("tensor", lambda h, g=g, c=c:
                          h.matmul(pb[bank][:, 2 + g:3 + g], r2b_t[:, c, g * 128:(g + 1) * 128], onesb[:, 0:1],
                                   start=(c == 0 and g == 0), stop=(c == DC - 1 and g == 1)),
                          reads=(r2b_cn(c), "onesb"), writes=(PB[bank],), accum=(not (c == 0 and g == 0)))
            pl.op("vector", lambda h: h.tensor_scalar(lncols[:, 0:2], pb[bank][:, 0:2], CMEAN_D, None, ALU.mult),
                  reads=(PB[bank],), writes=("ln_m",))
            pl.op("vector", lambda h: h.tensor_tensor(lncols[:, 6:8], lncols[:, 0:2], lncols[:, 0:2], ALU.mult),
                  reads=("ln_m",), writes=("ln_msq",))
            pl.op("vector", lambda h: h.scalar_tensor_tensor(lncols[:, 6:8], pb[bank][:, 2:4], CMEAN_D,
                                                             lncols[:, 6:8], ALU.mult, ALU.subtract),
                  reads=(PB[bank], "ln_msq"), writes=("ln_var",))
            pl.op("vector", lambda h: h.tensor_scalar(lncols[:, 6:8], lncols[:, 6:8], 1e-5, None, ALU.add),
                  reads=("ln_var",), writes=("ln_varp",))
            pl.op("scalar", lambda h: h.activation(lncols[:, 6:8], lncols[:, 6:8], AF.Sqrt, bias=0.0, scale=1.0),
                  reads=("ln_varp",), writes=("ln_sd",))
            if dsf is not None:
                act_dummy(dsf)
            pl.op("vector", lambda h: h.reciprocal(lncols[:, 2:6:2], lncols[:, 6:8]),
                  reads=("ln_sd",), writes=("ln_rstd",))
            pl.op("vector", lambda h: h.tensor_tensor(lncols[:, 3:7:2], lncols[:, 0:2], lncols[:, 2:6:2], ALU.mult),
                  reads=("ln_m", "ln_rstd"), writes=("ln_mrstd",))
            for g in range(2):
                pl.op("tensor", lambda h, g=g:
                      h.transpose(pb[bank][0:2, 8 + g * 130:8 + g * 130 + 128], lncols[:, 2 + 2 * g:4 + 2 * g], identt[:]),
                      reads=("ln_rstd", "ln_mrstd", "identt"), writes=(PB[bank],), accum=(g > 0))
            for g in range(2):
                pl.op("vector", lambda h, g=g:
                      h.tensor_copy(lnrow2[0:2, g * 128:(g + 1) * 128], pb[bank][0:2, 8 + g * 130:8 + g * 130 + 128]),
                      reads=(PB[bank],), writes=("lnrow2",))
            pl.op("tensor", lambda h: h.matmul(pb[7][:, 0:TT], ones1[:], lnrow2[0:1, 0:TT], start=True, stop=True),
                  reads=("lnrow2", "ones1"), writes=(PB[7],))
            pl.op("tensor", lambda h: h.matmul(pb[7][:, TT:2 * TT], sel2[:, :], lnrow2[0:2, 0:TT], start=True, stop=True),
                  reads=("lnrow2", "sel2"), writes=(PB[7],))

        def a_front(i):
            s = i % 2
            s3 = i % 3
            a0 = i * TT
            pl.op("scalar", lambda h, s=s, a0=a0:
                  h.dma_start(out=xtm[s][:], in_=x_d[a0:a0 + TT, :].rearrange("(g p) d -> p g d", p=128)),
                  writes=(f"xtm{s}",), sem=f"dx{s}", dma=True)
            for g in range(2):
                for cq in range(2):
                    for j in range(4):
                        c = cq * 4 + j
                        pl.op("tensor", lambda h, s=s, g=g, c=c, j=j:
                              h.transpose(pb[0][:, j * 128:(j + 1) * 128],
                                          xtm[s][:, g, c * 128:(c + 1) * 128], identt[:]),
                              reads=(f"xtm{s}", "identt"), writes=(PB[0],), accum=(j > 0))
                    pview = pb[0][:, 0:512].rearrange("p (j n) -> p j n", n=128)
                    pl.op("scalar", lambda h, s=s, g=g, cq=cq, pview=pview:
                          h.activation(xfm[s3][:, cq * 4:(cq + 1) * 4, g * 128:(g + 1) * 128],
                                       pview, AF.Copy, bias=0.0, scale=1.0),
                          reads=(PB[0],), writes=(f"xfm{s3}",))
                    pl.op("scalar", lambda h, s=s, g=g, cq=cq, pview=pview:
                          h.activation(xbf[s3][:, cq * 4:(cq + 1) * 4, g * 128:(g + 1) * 128],
                                       pview, AF.Copy, bias=0.0, scale=1.0),
                          reads=(PB[0],), writes=(f"xbf{s3}",))
            for mc in range(DC):
                qz = 2 + (mc % 2)
                qh = 4 + (mc % 2)
                for kc in range(DC):
                    pl.op("tensor", lambda h, kc=kc, mc=mc, qz=qz, s=s:
                          h.matmul(pb[qz][:, 0:TT], WzT[:, kc, mc * 128:(mc + 1) * 128], xbf[s3][:, kc, :],
                                   start=(kc == 0), stop=(kc == DC - 1)),
                          reads=(f"xbf{s3}", "WzT"), writes=(PB[qz],), accum=(kc > 0))
                pl.op("scalar", lambda h, mc=mc, qz=qz, s=s:
                      h.activation(zt[s3][:, mc, :], pb[qz][:, 0:TT], AF.Sigmoid,
                                   bias=bzt[:, mc:mc + 1], scale=1.0),
                      reads=(PB[qz], f"bzt{mc}"), writes=(f"zt{s3}_{mc}",))
                for kc in range(DC):
                    pl.op("tensor", lambda h, kc=kc, mc=mc, qh=qh, s=s:
                          h.matmul(pb[qh][:, 0:TT], WhT[:, kc, mc * 128:(mc + 1) * 128], xbf[s3][:, kc, :],
                                   start=(kc == 0), stop=(kc == DC - 1)),
                          reads=(f"xbf{s3}", "WhT"), writes=(PB[qh],), accum=(kc > 0))
                pl.op("vector", lambda h, mc=mc, qh=qh, s=s:
                      h.scalar_tensor_tensor(bt[s3][:, mc, :], pb[qh][:, 0:TT], bht[:, mc:mc + 1],
                                             zt[s3][:, mc, :], ALU.add, ALU.mult),
                      reads=(PB[qh], f"bht{mc}", f"zt{s3}_{mc}"), writes=(f"bt{s3}_{mc}",))
                pl.op("gpsimd", lambda h, mc=mc, s3=s3:
                      h.tensor_scalar(sc1[s3][:, mc, :], zt[s3][:, mc, :], -1.0, 1.0, ALU.mult, ALU.add),
                      reads=(f"zt{s3}_{mc}", f"bt{s3}_{mc}"), writes=(f"sc1{s3}_{mc}",))
                pl.op("gpsimd", lambda h, mc=mc, s3=s3:
                      h.tensor_scalar(zt[s3][:, mc, :], sc1[s3][:, mc, :], 1.0, 1e-8, ALU.mult, ALU.add),
                      reads=(f"sc1{s3}_{mc}",), writes=(f"zt{s3}_{mc}",))
            act_dummy(AF.Square)

        def a_back(i, upto=99):
            s = i % 2
            s3 = i % 3
            a0 = i * TT
            for c in range(DC):
                init = 0.0 if i == 0 else ht[1 - s][:, c, TT - 1:TT]
                rd = (f"zt{s3}_{c}", f"bt{s3}_{c}") + (() if i == 0 else (f"ht{1 - s}",))
                pl.op("vector", lambda h, s=s, c=c, init=init:
                      h.tensor_tensor_scan(ht[s][:, c, :], zt[s3][:, c, :], bt[s3][:, c, :],
                                           init, ALU.mult, ALU.add),
                      reads=rd, writes=(f"ht{s}",))
                pl.op("gpsimd", lambda h, s=s, c=c:
                      h.tensor_tensor(sc1[s3][:, c, :], xfm[s3][:, c, :], ht[s][:, c, :], ALU.add),
                      reads=(f"ht{s}", f"xfm{s3}"), writes=(f"sc1{s3}_{c}",))
                pl.op("scalar", lambda h, s=s, s3=s3, c=c:
                      h.activation(rbt[s][:, c, :], sc1[s3][:, c, :], AF.Copy, bias=0.0, scale=1.0),
                      reads=(f"sc1{s3}_{c}",), writes=(f"rbt{s}", f"rbt{s}_{c}"))
                pl.op("scalar", lambda h, s=s, s3=s3, c=c:
                      h.activation(r2bt[s][:, c, :], rbt[s][:, c, :], AF.Square, bias=0.0, scale=1.0),
                      reads=(f"rbt{s}_{c}",), writes=(f"r2bt{s}", f"r2bt{s}_{c}"))

            act_dummy(AF.Sqrt)
            ln_stats_apply(rbt[s], r2bt[s], f"rbt{s}", f"r2bt{s}", 6, dsf=AF.Sigmoid,
                           rb_cn=lambda c, s=s: f"rbt{s}_{c}", r2b_cn=lambda c, s=s: f"r2bt{s}_{c}")
            pl.op("vector", lambda h: h.tensor_copy(bmrs[:], pb[7][:, TT:2 * TT]),
                  reads=(PB[7],), writes=("bmrs",))
            for c in range(DC):
                pl.op("vector", lambda h, c=c, s=s:
                      h.tensor_tensor(bt[s3][:, c, :], sc1[s3][:, c, :], pb[7][:, 0:TT], ALU.mult),
                      reads=(f"sc1{s3}_{c}", PB[7]), writes=(f"bt{s3}_{c}",))
                pl.op("gpsimd", lambda h, c=c, s=s:
                      h.tensor_tensor(yt[:, c, :], bt[s3][:, c, :], bmrs[:], ALU.subtract),
                      reads=(f"bt{s3}_{c}", "bmrs"), writes=("yta",))
            pl.op("sync", lambda h, a0=a0: h.dma_start(out=yscr_r[:, :, a0:a0 + TT], in_=yt[:]),
                  reads=("yta",), writes=(f"yscr{i}",), sem=f"dy{i % 2}", dma=True)

        upto = 99
        only1 = False
        if stage.startswith("A") and len(stage) > 1:
            upto = int(stage[1:])
            only1 = True
        a_front(0)
        if only1:
            a_back(0, upto)
        else:
            if NT > 1:
                a_front(1)
            for i in range(NT):
                if i + 2 < NT:
                    a_front(i + 2)
                a_back(i)
                nun = (NSU * (i + 1)) // NT - (NSU * i) // NT
                for _ in range(nun):
                    if stat_units:
                        stat_unit(*stat_units.pop(0))

        stAw.close()
        stA.close()
        pl.barrier()
        if stage.startswith("A"):
            with nc.Block() as block:
                pl.emit_all(block)
            return nc

        stBr = ExitStack()
        wq1T = stBr.enter_context(nc.sbuf_tensor("wq1T", [128, DC, HD], FP8))
        wq2T = stBr.enter_context(nc.sbuf_tensor("wq2T", [128, HC, D], FP8))
        diag = stBr.enter_context(nc.sbuf_tensor("diag", [128, HC, 3, 128], BF16))

        for hc in range(HC):
            for k in range(3):
                pl.op("vector",
                      lambda h, hc=hc, k=k: h.tensor_scalar(diag[:, hc, k, :], identt[:],
                                                            cwt[:, hc, k:k + 1], None, ALU.mult),
                      reads=("identt", f"cwt{hc}"), writes=("diag",))

        with nc.sbuf_tensor("wsl0b", [128, 4096], F32) as wsl0b, \
             nc.sbuf_tensor("wsl1b", [128, 4096], F32) as wsl1b, \
             nc.sbuf_tensor("qi8", [128, 2, 512], I8) as qi8:
            wslb = [wsl0b, wsl1b]

            def prep_w(w_src, nchunk, ncols, wqT, wqn, sc_i):
                nkc = ncols // 128
                if sc_i != 0:
                    for ci in range(nchunk):
                        s = ci % 2
                        pl.op("sync", lambda h, s=s, ci=ci, ncols=ncols, w_src=w_src:
                              h.dma_start(out=wslb[s][:, 0:ncols], in_=w_src[ci * 128:(ci + 1) * 128, :]),
                              writes=(f"wslb{s}",), sem=f"dwp{s}", dma=True)
                        v1 = wslb[s][:, 0:ncols].rearrange("p (a b) -> p a b", b=128)
                        pl.op("vector", lambda h, s=s, v1=v1, nkc=nkc:
                              h.tensor_reduce(wtmp[:, 0:nkc], v1, AX.X, ALU.add, apply_absolute_value=True),
                              reads=(f"wslb{s}",), writes=("wtmp",))
                        pl.op("vector", lambda h, ci=ci, nkc=nkc:
                              h.tensor_reduce(wsum[:, ci:ci + 1], wtmp[:, 0:nkc], AX.X, ALU.add),
                              reads=("wtmp",), writes=("wsum",))
                pl.op("vector", lambda h, nchunk=nchunk:
                      h.tensor_reduce(cols[:, 0:1], wsum[:, 0:nchunk], AX.X, ALU.add),
                      reads=("wsum",), writes=("colsum",))
                pl.op("tensor", lambda h: h.matmul(pb[0][0:1, 0:1], onesf[:, 0:1], cols[:, 0:1],
                                                   start=True, stop=True),
                      reads=("colsum", "onesf"), writes=(PB[0],))
                pl.op("vector", lambda h: h.tensor_scalar(scl1[0:1, 0:1], pb[0][0:1, 0:1], CMEAN_W, 1e-5,
                                                          ALU.mult, ALU.max),
                      reads=(PB[0],), writes=("s_mean",))
                pl.op("vector", lambda h: h.reciprocal(scl1[0:1, 1:2], scl1[0:1, 0:1]),
                      reads=("s_mean",), writes=("s_scale",))
                pl.op("tensor", lambda h: h.matmul(pb[0][:, 8:10], ones1[:], scl1[0:1, 0:2],
                                                   start=True, stop=True),
                      reads=("s_mean", "s_scale", "ones1"), writes=(PB[0],))
                pl.op("scalar", lambda h, sc_i=sc_i: h.activation(scol[:, sc_i:sc_i + 2], pb[0][:, 8:10],
                                                                  AF.Copy, bias=0.0, scale=1.0),
                      reads=(PB[0],), writes=(f"scol{sc_i}",))
                for ci in range(nchunk):
                    s = ci % 2
                    pl.op("sync", lambda h, s=s, ci=ci, ncols=ncols, w_src=w_src:
                          h.dma_start(out=wslb[s][:, 0:ncols], in_=w_src[ci * 128:(ci + 1) * 128, :]),
                          writes=(f"wslb{s}",), sem=f"dwp{s}", dma=True)
                    for k4 in range(nkc // 4):
                        q = k4 % 2
                        for j in range(4):
                            kc = k4 * 4 + j
                            pl.op("tensor", lambda h, s=s, kc=kc, q=q, j=j:
                                  h.transpose(pb[1 + q][:, j * 128:(j + 1) * 128],
                                              wslb[s][:, kc * 128:(kc + 1) * 128], identt[:]),
                                  reads=(f"wslb{s}", "identt"), writes=(PB[1 + q],), accum=(j > 0))
                        pl.op("scalar", lambda h, q=q, sc_i=sc_i:
                              h.activation(qi8[:, q, :], pb[1 + q][:, 0:512], AF.Copy,
                                           bias=0.0, scale=scol[:, sc_i + 1:sc_i + 2]),
                              reads=(PB[1 + q], f"scol{sc_i}"), writes=(f"qi8{q}",))
                        pl.op("vector", lambda h, q=q, wqT=wqT, k4=k4, ci=ci, wqn=wqn:
                              h.tensor_scalar(wqT[:, k4 * 4:(k4 + 1) * 4, ci * 128:(ci + 1) * 128],
                                              qi8[:, q, :].rearrange("p (j n) -> p j n", n=128),
                                              1.0, -1.0, ALU.min, ALU.max),
                              reads=(f"qi8{q}",), writes=(wqn,))

            prep_w(w1_d, HC, 1024, wq1T, "wq1T", 0)
            prep_w(w2_d, DC, 4096, wq2T, "wq2T", 2)
        stPrep.close()

        pl.barrier()
        if stage == "prep2":
            stBr.close()
            with nc.Block() as block:
                pl.emit_all(block)
            return nc

        stBw = ExitStack()
        ytile2 = [stBw.enter_context(nc.sbuf_tensor(f"ytile{s}", [128, DC, NW], F32)) for s in range(3)]
        mxt = stBw.enter_context(nc.sbuf_tensor("mxt", [128, NW], F32))
        xq8 = stBw.enter_context(nc.sbuf_tensor("xq8", [128, 2, NW], I8))
        xqbf = stBw.enter_context(nc.sbuf_tensor("xqbf", [128, DC, NW], BF16))
        hbf = stBw.enter_context(nc.sbuf_tensor("hbf", [128, HC, NW], BF16))
        vt = stBw.enter_context(nc.sbuf_tensor("vt", [128, HC, TT], F32))
        vq8 = stBw.enter_context(nc.sbuf_tensor("vq8", [128, 2, 2 * TT], I8))
        vqbf = stBw.enter_context(nc.sbuf_tensor("vqbf", [128, HC, TT], BF16))
        ftl = [stBw.enter_context(nc.sbuf_tensor(f"ftb{s}", [128, DC, TT], F32)) for s in range(2)]
        r2t = stBw.enter_context(nc.sbuf_tensor("r2t", [128, DC, TT], F32))
        outfm = r2t
        outtm = stBw.enter_context(nc.sbuf_tensor("outtm", [128, 2, 512], F32))
        bm1sb = stBw.enter_context(nc.sbuf_tensor("bm1sb", [128, NW], F32))
        bdq2sb = stBw.enter_context(nc.sbuf_tensor("bdq2sb", [128, TT], F32))
        bmrs2 = stBw.enter_context(nc.sbuf_tensor("bmrs2", [128, TT], F32))

        def quant_rows(nw, mean_sc_i):
            ngr = (nw + 127) // 128
            for gi in range(ngr):
                lo = gi * 128
                w = min(nw, lo + 128) - lo
                pl.op("tensor", lambda h, lo=lo, w=w, gi=gi:
                      h.transpose(pb[4][0:w, gi * 128:gi * 128 + 128], mxt[:, lo:lo + w], identt[:]),
                      reads=("mxt", "identt"), writes=(PB[4],), accum=(gi > 0))
            nfull = nw // 128
            pl.op("vector", lambda h, nfull=nfull:
                  h.tensor_reduce(cols[:, 4:4 + nfull],
                                  pb[4][:, 0:nfull * 128].rearrange("p (g j) -> p g j", j=128),
                                  AX.X, ALU.max),
                  reads=(PB[4],), writes=("qcm",))
            if nw % 128:
                wl = nw % 128
                pl.op("vector", lambda h, nfull=nfull, wl=wl:
                      h.tensor_reduce(cols[0:wl, 4 + nfull:5 + nfull],
                                      pb[4][0:wl, nfull * 128:nfull * 128 + 128], AX.X, ALU.max),
                      reads=(PB[4],), writes=("qcm",))
            pl.op("vector", lambda h, ngr=ngr:
                  h.tensor_scalar(cols[:, 8:8 + ngr], cols[:, 4:4 + ngr], 1e-5, None, ALU.max),
                  reads=("qcm",), writes=("qmc",))
            pl.op("vector", lambda h, ngr=ngr: h.reciprocal(cols[:, 12:12 + ngr], cols[:, 8:8 + ngr]),
                  reads=("qmc",), writes=("qrec",))
            pl.op("vector", lambda h, ngr=ngr:
                  h.tensor_scalar(cols[:, 0:2 * ngr:2], cols[:, 12:12 + ngr], 127.0, None, ALU.mult),
                  reads=("qrec",), writes=("qsc",))
            pl.op("vector", lambda h, ngr=ngr, mean_sc_i=mean_sc_i:
                  h.tensor_scalar(cols[:, 1:2 * ngr + 1:2], cols[:, 8:8 + ngr], C127I,
                                  scol[:, mean_sc_i:mean_sc_i + 1], ALU.mult, ALU.mult),
                  reads=("qmc", f"scol{mean_sc_i}"), writes=("qdq",))
            for gi in range(ngr):
                lo = gi * 128
                w = min(nw, lo + 128) - lo
                pl.op("tensor", lambda h, gi=gi, w=w:
                      h.transpose(pb[4][0:2, 384 + 0:384 + w] if False else pb[4][0:2, gi * 130:gi * 130 + w],
                                  cols[0:w, 2 * gi:2 * gi + 2], identt[0:w, 0:w]),
                      reads=("qsc", "qdq", "identt"), writes=(PB[4],), accum=(gi > 0))
            for gi in range(ngr):
                lo = gi * 128
                w = min(nw, lo + 128) - lo
                pl.op("vector", lambda h, gi=gi, lo=lo, w=w:
                      h.tensor_copy(qrow2[0:2, lo:lo + w], pb[4][0:2, gi * 130:gi * 130 + w]),
                      reads=(PB[4],), writes=("qrow2",))

        def b_load(i):
            s = i % 3
            ytile = ytile2[s]
            yr = f"ytile{s}"
            if i == 0:
                pl.op("vector", lambda h, ytile=ytile: h.memset(ytile[:, :, 0:1], 0.0), writes=(yr,))
                pl.op("scalar", lambda h, ytile=ytile: h.dma_start(out=ytile[:, :, 1:NW], in_=yscr_r[:, :, 0:TT + 1]),
                      reads=("yscr0", "yscr1"), writes=(yr,), sem=f"dyr{s}", dma=True)
            elif i == NT - 1:
                pl.op("vector", lambda h, ytile=ytile: h.memset(ytile[:, :, NW - 1:NW], 0.0), writes=(yr,))
                pl.op("scalar", lambda h, a0=i * TT, ytile=ytile: h.dma_start(out=ytile[:, :, 0:NW - 1], in_=yscr_r[:, :, a0 - 1:T]),
                      reads=(f"yscr{i - 1}", f"yscr{i}"), writes=(yr,), sem=f"dyr{s}", dma=True)
            else:
                pl.op("scalar", lambda h, a0=i * TT, ytile=ytile: h.dma_start(out=ytile[:], in_=yscr_r[:, :, a0 - 1:a0 + TT + 1]),
                      reads=(f"yscr{i - 1}", f"yscr{i}", f"yscr{i + 1}"), writes=(yr,), sem=f"dyr{s}", dma=True)

        def q2_red_half(half):
            pl.op("vector", lambda h, half=half:
                  h.tensor_reduce(pb[6][:, half * TT:(half + 1) * TT],
                                  vt[:, half * 16:(half + 1) * 16, :].rearrange("p c t -> p t c"),
                                  AX.X, ALU.max, apply_absolute_value=True),
                  reads=tuple(f"vt{hc}" for hc in range(half * 16, (half + 1) * 16)),
                  writes=(PB[6],))

        def q2_combine():
            pl.op("vector", lambda h:
                  h.tensor_reduce(mxt[:, 0:TT], pb[6][:, 0:512].rearrange("p (c t) -> p t c", t=TT),
                                  AX.X, ALU.max),
                  reads=(PB[6],), writes=("mxt",))

        def q2_bcast(i):
            pl.op("tensor", lambda h: h.matmul(pb[7][:, 0:TT], ones1[:], qrow2[0:1, 0:TT], start=True, stop=False),
                  reads=("qrow2", "ones1"), writes=(PB[7],))
            pl.op("tensor", lambda h: h.matmul(pb[7][:, TT:2 * TT], ones1[:], qrow2[0:1, 0:TT], start=False, stop=True),
                  reads=("qrow2", "ones1"), writes=(PB[7],), accum=True)
            pl.op("tensor", lambda h: h.matmul(pb[4][:, 0:TT], sel2[:, :], qrow2[0:2, 0:TT], start=True, stop=True),
                  reads=("qrow2", "sel2"), writes=(PB[4],))
            pl.op("scalar", lambda h: h.activation(bdq2sb[:], pb[4][:, 0:TT], AF.Copy, bias=0.0, scale=1.0),
                  reads=(PB[4],), writes=("bdq2sb",))

        def plan_vq(i, hp):
            pl.op("vector", lambda h, hp=hp:
                  h.tensor_tensor(vq8[:, hp % 2, :].rearrange("p (c t) -> p c t", t=TT),
                                  vt[:, 2 * hp:2 * hp + 2, :],
                                  pb[7][:, 0:2 * TT].rearrange("p (c t) -> p c t", t=TT), ALU.mult),
                  reads=(f"vt{2 * hp}", f"vt{2 * hp + 1}", PB[7]), writes=(f"vq8_{hp % 2}",))
            pl.op("scalar", lambda h, hp=hp:
                  h.activation(vqbf[:, 2 * hp:2 * hp + 2, :],
                               vq8[:, hp % 2, :].rearrange("p (c t) -> p c t", t=TT),
                               AF.Copy, bias=0.0, scale=1.0),
                  reads=(f"vq8_{hp % 2}",),
                  writes=(f"vqbf{2 * hp}", f"vqbf{2 * hp + 1}") + (("vqbf_done",) if hp == HC // 2 - 1 else ()))

        def bt_m1(i, vq_of=None):
            a0 = i * TT
            s = i % 3
            ytile = ytile2[s]
            yr = f"ytile{s}"
            pl.op("vector", lambda h, ytile=ytile: h.tensor_reduce(mxt[:, 0:NW], ytile[:].rearrange("p c t -> p t c"),
                                                      AX.X, ALU.max, apply_absolute_value=True),
                  reads=(yr,), writes=("mxt",))
            quant_rows(NW, 0)
            pl.op("tensor", lambda h: h.matmul(pb[5][:, 0:NW], ones1[:], qrow2[0:1, 0:NW], start=True, stop=True),
                  reads=("qrow2", "ones1"), writes=(PB[5],))
            pl.op("tensor", lambda h: h.matmul(pb[4][:, 0:NW], sel2[:, :], qrow2[0:2, 0:NW], start=True, stop=True),
                  reads=("qrow2", "sel2"), writes=(PB[4],))
            pl.op("scalar", lambda h: h.activation(bm1sb[:], pb[4][:, 0:NW], AF.Copy, bias=0.0, scale=1.0),
                  reads=(PB[4],), writes=("bm1sb",))
            for c in range(DC):
                pl.op("vector", lambda h, c=c, ytile=ytile: h.tensor_tensor(xq8[:, c % 2, :], ytile[:, c, :], pb[5][:, 0:NW], ALU.mult),
                      reads=(yr, PB[5]), writes=(f"xq8_{c % 2}",))
                pl.op("scalar", lambda h, c=c: h.activation(xqbf[:, c, :], xq8[:, c % 2, :], AF.Copy, bias=0.0, scale=1.0),
                      reads=(f"xq8_{c % 2}",), writes=(f"xqbf{c}",) + (("xqbf_done",) if c == DC - 1 else ()))
            for hc in range(HC):
                q = hc % 2
                for kc in range(DC):
                    pl.op("tensor", lambda h, hc=hc, kc=kc, q=q:
                          h.matmul(pb[q][:, 0:NW], wq1T[:, kc, hc * 128:(hc + 1) * 128], xqbf[:, kc, :],
                                   start=(kc == 0), stop=(kc == DC - 1)),
                          reads=(f"xqbf{kc}", "wq1T"), writes=(PB[q],), accum=(kc > 0))
                pl.op("vector", lambda h, hc=hc, q=q:
                      h.tensor_tensor(hbf[:, hc, :], pb[q][:, 0:NW], bm1sb[:], ALU.mult),
                      reads=(PB[q], "bm1sb"), writes=(f"hbf{hc}",))
                if vq_of is not None and hc % 2 == 1:
                    plan_vq(vq_of, hc // 2)
            if i == 0 or i == NT - 1:
                col = slice(0, 1) if i == 0 else slice(NW - 1, NW)
                pl.op("vector", lambda h, col=col: h.memset(hbf[:, :, col], 0.0),
                      writes=tuple(f"hbf{hc}" for hc in range(HC)))
            for hc in range(HC):
                q = hc % 4
                for k in range(3):
                    pl.op("tensor", lambda h, hc=hc, k=k, q=q:
                          h.matmul(pb[q][:, 0:TT], diag[:, hc, k, :], hbf[:, hc, k:k + TT],
                                   start=(k == 0), stop=(k == 2)),
                          reads=(f"hbf{hc}", "diag"), writes=(PB[q],), accum=(k > 0))
                pl.op("scalar", lambda h, hc=hc, q=q:
                      h.activation(vt[:, hc, :], pb[q][:, 0:TT], AF.Silu, bias=cbt[:, hc:hc + 1], scale=1.0),
                      reads=(PB[q], f"cbt{hc}"), writes=(f"vt{hc}",))
            q2_red_half(0)

        def bt_w2(i, standalone_vq=False, q2red=False):
            sf = i % 2
            ft = ftl[sf]
            if standalone_vq:
                for hp in range(HC // 2):
                    plan_vq(i, hp)
            for dc in range(DC):
                q = (5, 4, 2, 3, 5, 4, 2, 3)[dc]
                for kc in range(HC):
                    pl.op("tensor", lambda h, dc=dc, kc=kc, q=q:
                          h.matmul(pb[q][:, 0:TT], wq2T[:, kc, dc * 128:(dc + 1) * 128], vqbf[:, kc, :],
                                   start=(kc == 0), stop=(kc == HC - 1)),
                          reads=(f"vqbf{kc}", "wq2T"), writes=(PB[q],), accum=(kc > 0))
                pl.op("vector", lambda h, dc=dc, q=q, ft=ft:
                      h.tensor_tensor(ft[:, dc, :], pb[q][:, 0:TT], bdq2sb[:], ALU.mult),
                      reads=(PB[q], "bdq2sb"), writes=(f"ft{sf}_{dc}",))
                if dc == 3 and q2red:
                    q2_red_half(1)
                    q2_combine()

        def bt_back(i):
            a0 = i * TT
            s = i % 3
            sf = i % 2
            ft = ftl[sf]
            ytile = ytile2[s]
            yr = f"ytile{s}"
            fdeps = tuple(f"ft{sf}_{dc}" for dc in range(DC))
            r2c = tuple(f"r2tc{c}" for c in range(DC))
            pl.op("gpsimd", lambda h, ytile=ytile, ft=ft: h.tensor_tensor(r2t[:], ytile[:, :, 1:NW - 1], ft[:], ALU.add),
                  reads=fdeps + (yr,), writes=r2c)
            pl.op("scalar", lambda h: h.activation(xqbf[:, :, 0:TT], r2t[:], AF.Copy, bias=0.0, scale=1.0),
                  reads=r2c, writes=("rb2", "xqbf_done") + tuple(f"xqbf{c}" for c in range(DC)))
            act_dummy(AF.Sqrt)
            pl.op("gpsimd", lambda h: h.tensor_tensor(vqbf[:, 0:DC, :], r2t[:], r2t[:], ALU.mult),
                  reads=r2c, writes=("r2b2", "vqbf_done") + tuple(f"vqbf{c}" for c in range(DC)))
            ln_stats_apply(xqbf[:, :, 0:TT].rearrange("p c t -> p c t") if False else xqbf, vqbf, "rb2", "r2b2", 6, dsf=AF.Silu)
            pl.op("vector", lambda h: h.tensor_copy(bmrs2[:], pb[7][:, TT:2 * TT]),
                  reads=(PB[7],), writes=("bmrs2",))
            for c in range(DC):
                pl.op("vector", lambda h, c=c, ft=ft: h.tensor_tensor(ft[:, c, :], r2t[:, c, :], pb[7][:, 0:TT], ALU.mult),
                      reads=(f"r2tc{c}", PB[7]), writes=(f"ft{sf}_{c}",))
                pl.op("gpsimd", lambda h, c=c, ft=ft: h.tensor_tensor(outfm[:, c, :], ft[:, c, :], bmrs2[:],
                                                               ALU.subtract),
                      reads=(f"ft{sf}_{c}", "bmrs2"), writes=(f"r2tc{c}",))
            for g in range(2):
                for cq in range(2):
                    r = g * 2 + cq
                    qb = 4 if r % 2 == 0 else 5
                    for j in range(4):
                        c = cq * 4 + j
                        pl.op("tensor", lambda h, g=g, c=c, j=j, qb=qb:
                              h.transpose(pb[qb][:, j * 128:(j + 1) * 128], outfm[:, c, g * 128:(g + 1) * 128],
                                          identt[:]),
                              reads=(f"r2tc{c}", "identt"), writes=(PB[qb],), accum=(j > 0))
                    pl.op("scalar", lambda h, r=r, qb=qb:
                          h.activation(outtm[:, r % 2, :].rearrange("p (j n) -> p j n", n=128),
                                       pb[qb][:, 0:512].rearrange("p (j n) -> p j n", n=128),
                                       AF.Copy, bias=0.0, scale=1.0),
                          reads=(PB[qb],), writes=(f"outtm{r % 2}",))
                    pl.op("sync", lambda h, a0=a0, g=g, cq=cq, r=r:
                          h.dma_start(out=out_d[a0 + g * 128:a0 + (g + 1) * 128, cq * 512:(cq + 1) * 512],
                                      in_=outtm[:, r % 2, :]),
                          reads=(f"outtm{r % 2}",), writes=(f"outd{i}_{g}_{cq}",), sem=f"dow{r % 2}", dma=True)

        b_load(0)
        if NT > 1:
            b_load(1)
        bt_m1(0)
        q2_red_half(1)
        q2_combine()
        quant_rows(TT, 2)
        for i in range(NT):
            q2_bcast(i)
            if i + 1 < NT:
                bt_m1(i + 1, vq_of=i)
                bt_w2(i, q2red=True)
                quant_rows(TT, 2)
            else:
                bt_w2(i, standalone_vq=True)
            if i > 0:
                bt_back(i - 1)
            if i + 2 < NT:
                b_load(i + 2)
        bt_back(NT - 1)

        stBw.close()
        stBr.close()
        pl.op("sync", None, extra_deps=[("dow0", pl.cum["dow0"]), ("dow1", pl.cum["dow1"])])

        with nc.Block() as block:
            pl.emit_all(block)
    return nc


_NC_CACHE = {}


def kernel(**inputs):
    xs = np.ascontiguousarray(np.asarray(inputs["L0_attn_x"], dtype=np.float32))
    ident = np.eye(128, dtype=np.float32)
    sel2 = np.zeros((2, 128), dtype=np.float32)
    sel2[1, :] = 1.0
    names = ["L0_attn_Wz", "L0_attn_bz", "L0_attn_Wh", "L0_attn_bh",
             "L0_ffn_w1", "L0_ffn_conv_w", "L0_ffn_conv_b", "L0_ffn_w2"]
    ws = {n: np.ascontiguousarray(np.asarray(inputs[n], dtype=np.float32)) for n in names}
    if "nc" not in _NC_CACHE:
        _NC_CACHE["nc"] = build_program()
    nc = _NC_CACHE["nc"]
    in_maps = []
    for c in range(8):
        m = {"L0_attn_x": xs[c], "ident": ident, "sel2": sel2}
        m.update(ws)
        in_maps.append(m)
    res = run_bass_kernel_spmd(nc, in_maps, core_ids=list(range(8)))
    out = np.stack([np.asarray(r["out"]) for r in res.results], axis=0).astype(np.float32)
    return out
